# version 01
# baseline (speedup 1.0000x reference)
import numpy as np
import contextlib
import concourse.bass as bass
import concourse.mybir as mybir
from concourse.bass_utils import run_bass_kernel_spmd

F32 = mybir.dt.float32
BF16 = mybir.dt.bfloat16
I32 = mybir.dt.int32
AF = mybir.ActivationFunctionType
ALU = mybir.AluOpType
AX = mybir.AxisListType.X

L, DM, DI, DTR, DS, DC, INTER, NLUT = 8192, 2048, 4096, 128, 16, 4, 5504, 16
EPS = 1e-5
TC = 1024
NKC = L // TC
NDT = DI // 128
NFT = DM // 128
NIT = INTER // 128
NTT = TC // 128

ENGS = ("sync", "tensor", "vector", "scalar", "gpsimd")


class B:

    SAFE = True

    def __init__(self):
        self.q = {g: [] for g in ENGS}
        self.cnt = {}
        self.owner = {}
        self.sems = {}
        self.kreg = {}

    def _c(self, name):
        return self.cnt.get(name, 0)

    def op(self, g, fn, inc=1, selfwait=True):
        sems = self.sems
        if self.SAFE and selfwait and self._c(g) > 0:
            pv = self._c(g)

            def wrun(eng, pv=pv, g=g):
                eng.wait_ge(sems[g], pv)
            self.q[g].append(wrun)
        self.cnt[g] = self._c(g) + inc
        self.owner[g] = g

        def run(eng, fn=fn, g=g, inc=inc):
            fn(eng).then_inc(sems[g], inc)
        self.q[g].append(run)
        return self.cnt[g]

    def silent(self, g, fn):
        self.q[g].append(fn)

    def wait(self, g, src, val=None):
        if val is None:
            val = self._c(src)
        sems = self.sems

        def run(eng, src=src, val=val):
            if val > 0:
                eng.wait_ge(sems[src], val)
        self.q[g].append(run)

    def dma(self, g, sem, out, in_, **kw):
        self.cnt[sem] = self._c(sem) + 16
        self.owner[sem] = g
        sems = self.sems

        def run(eng, sem=sem):
            eng.dma_start(out=out, in_=in_, **kw).then_inc(sems[sem], 16)
        self.q[g].append(run)
        return self.cnt[sem]

    def dma_tr(self, g, sem, out, in_):
        self.cnt[sem] = self._c(sem) + 16
        self.owner[sem] = g
        sems = self.sems

        def run(eng, sem=sem):
            eng.dma_start_transpose(out, in_).then_inc(sems[sem], 16)
        self.q[g].append(run)
        return self.cnt[sem]

    def guarded(self, k, builder):
        gb = B.__new__(B)
        gb.q = {g: [] for g in ENGS}
        gb.cnt = dict(self.cnt)
        gb.owner = self.owner
        gb.sems = self.sems
        gb.kreg = self.kreg
        builder(gb)
        for g in ENGS:
            span = gb.q[g]
            deltas = [(name, gb._c(name) - self._c(name))
                      for name in gb.cnt
                      if gb.owner.get(name) == g and gb._c(name) > self._c(name)]
            if not span and not deltas:
                continue
            sems = self.sems
            kreg = self.kreg

            def run(eng, g=g, span=span, deltas=deltas, k=k):
                with eng.If_eq(kreg[g], k):
                    for fn in span:
                        fn(eng)
                if deltas:
                    with eng.Else():
                        for name, d in deltas:
                            eng.sem_inc(sems[name], d)
            self.q[g].append(run)
        self.cnt = gb.cnt


def build_program(nkc=NKC):
    nc = bass.Bass("TRN2", target_bir_lowering=False)

    def din(name, shape, dt=F32):
        return nc.dram_tensor(name, shape, dt, kind="ExternalInput").ap()

    xin = din("xin", [L, DM])
    xmy = din("xmy", [TC, DM])
    wx_blk = din("wx_blk", [128, NDT * 2048])
    wz_blk = din("wz_blk", [128, NDT * 2048])
    wot_blk = din("wot_blk", [128, NDT * 2048])
    wxp_dt = din("wxp_dt", [128, NDT * DTR])
    wxp_b = din("wxp_b", [128, NDT * DS])
    wxp_c = din("wxp_c", [128, NDT * DS])
    wdtt = din("wdtt", [DTR, DI])
    p_convw_in = din("p_convw_in", [128, NDT * DC])
    p_convb_in = din("p_convb_in", [128, NDT])
    p_dtb_in = din("p_dtb_in", [128, NDT])
    p_alog_in = din("p_alog_in", [128, NDT * DS])
    p_dvec_in = din("p_dvec_in", [128, NDT])
    lutv = din("lutv", [128, NLUT])
    g_blk = din("g_blk", [128, NIT * NFT * 128], I32)
    u_blk = din("u_blk", [128, NIT * NFT * 128], I32)
    d_blk = din("d_blk", [128, NFT * NIT * 128], I32)
    selmat = din("selmat", [DS, DS * 128], BF16)
    kown_in = din("kown_in", [1, 1], I32)

    yout = nc.dram_tensor("yout", [TC, DM], F32, kind="ExternalOutput").ap()

    def dscr(name, shape, dt):
        return nc.dram_tensor(name, shape, dt, kind="Internal")

    wxbf = dscr("wxbf", [128, NDT * 2048], BF16)
    xld_sc = dscr("xld_sc", [DI, TC], BF16)
    ln_tm = dscr("ln_tm", [TC, DM], BF16)
    ygate_sc = dscr("ygate_sc", [DI, TC], BF16)
    sz_sc = dscr("sz_sc", [DI, TC], BF16)
    wgbf = dscr("wgbf", [128, NIT * NFT * 128], BF16)
    wubf = dscr("wubf", [128, NIT * NFT * 128], BF16)
    wdbf = dscr("wdbf", [128, NFT * NIT * 128], BF16)
    h_sc = dscr("h_sc", [INTER, TC], BF16)
    fm_sc = dscr("fm_sc", [DM, TC], BF16)
    x2_sc = dscr("x2_sc", [TC, DM], F32)

    with contextlib.ExitStack() as stx:
        e = stx.enter_context

        def sb(name, p, f, dt):
            return e(nc.sbuf_tensor(name, [p, f], dt))

        ARENA_F = 20480
        arena = sb("arena", 128, ARENA_F, F32)
        a_bf = arena.bitcast(BF16)
        a_i32 = arena.bitcast(I32)

        def vbf(off_f32, free):
            return a_bf[0:128, off_f32 * 2:off_f32 * 2 + free]

        def vi32(off_f32, free):
            return a_i32[0:128, off_f32:off_f32 + free]

        ln1t = vbf(0, NFT * TC)
        b_bc = vbf(4096, DS * TC)
        c_bc = vbf(12288, DS * TC)
        cv_idx = [vi32(0, INTER), vi32(5504, INTER)]
        cv_ixb = vbf(11008, INTER)
        cv_acc = vbf(13760, INTER)
        cv_t = vbf(16512, INTER)
        ln2t = vbf(0, NFT * TC)
        dn_l = vbf(4096, NIT * 128)

        p_negA = sb("p_negA", 128, NDT * DS, F32)
        p_convw = sb("p_convw", 128, NDT * DC, F32)
        p_convb = sb("p_convb", 128, NDT, F32)
        p_dtb = sb("p_dtb", 128, NDT, F32)
        p_dvec = sb("p_dvec", 128, NDT, F32)
        p_sel = sb("p_sel", DS, DS * 128, BF16)
        p_lut = sb("p_lut", 128, NLUT, F32)
        p_kown = sb("p_kown", 1, 1, I32)
        carry = sb("carry", 128, NDT * DS, F32)
        dt_lhs = sb("dt_lhs", DTR, DI, BF16)
        xp_dt_l = sb("xp_dt_l", 128, NDT * DTR, BF16)
        xp_b_l = sb("xp_b_l", 128, NDT * DS, BF16)
        xp_c_l = sb("xp_c_l", 128, NDT * DS, BF16)
        dtlo_bf = sb("dtlo_bf", DTR, TC, BF16)
        bmat = sb("bmat", DS, TC, BF16)
        cmat = sb("cmat", DS, TC, BF16)
        xpre = sb("xpre", 128, NDT * (DC - 1), F32)

        t_ln = sb("t_ln", 128, DM, F32)
        t_lnsq = sb("t_lnsq", 128, DM, F32)
        t_st = sb("t_st", 128, 8, F32)
        t_lno = sb("t_lno", 128, DM, BF16)
        w_f32 = sb("w_f32", 128, DM, F32)
        w_bf = [sb("w_bf_%d" % i, 128, DM, BF16) for i in range(2)]
        t_xr = sb("t_xr", 128, TC + DC - 1, F32)
        t_cf = sb("t_cf", 128, TC, F32)
        t_sp = sb("t_sp", 128, TC, F32)
        t_dl = sb("t_dl", 128, TC, BF16)
        t_u = sb("t_u", 128, TC, BF16)
        t_xld = sb("t_xld", 128, TC, BF16)
        t_a = sb("t_a", 128, TC, F32)
        t_b = sb("t_b", 128, TC, BF16)
        t_h = sb("t_h", 128, TC, F32)
        t_ch = sb("t_ch", 128, TC, BF16)
        t_y = sb("t_y", 128, TC, BF16)
        t_yg = sb("t_yg", 128, TC, BF16)
        t_sz = sb("t_sz", 128, TC, BF16)
        t_ev = sb("t_ev", 128, TC, BF16)
        t_yt = [sb("t_yt_%d" % i, 128, TC, BF16) for i in range(2)]

        ps_mm = e(nc.psum_tensor("ps_mm", [128, TC], F32))
        ps_d = e(nc.psum_tensor("ps_d", [128, TC], F32))
        ps_bc = e(nc.psum_tensor("ps_bc", [128, TC], F32))
        ps_c = e(nc.psum_tensor("ps_c", [DS, TC], F32))

        P = B()

        def mm_pair(t, psum, lhsT, rhs, start, stop, pl=128, pbase=0):
            t.matmul(psum[pbase:pbase + pl, 0:512], lhsT, rhs[:, 0:512],
                     start=start, stop=stop)
            return t.matmul(psum[pbase:pbase + pl, 512:1024], lhsT,
                            rhs[:, 512:1024], start=start, stop=stop)

        P.dma("sync", "pld", p_sel[:, :], selmat)
        P.dma("sync", "pld", p_lut[:, :], lutv)
        P.dma("sync", "pld", p_kown[:, :], kown_in)
        P.dma("sync", "pld", p_convw[:, :], p_convw_in)
        P.dma("sync", "pld", p_convb[:, :], p_convb_in)
        P.dma("sync", "pld", p_dtb[:, :], p_dtb_in)
        P.dma("sync", "pld", p_dvec[:, :], p_dvec_in)
        P.dma("sync", "pld", p_negA[:, :], p_alog_in)

        for g in ENGS:
            P.wait(g, "pld")

            def mkload(g=g):
                def run(eng):
                    reg = eng.register("kreg_" + g).__enter__()
                    P.kreg[g] = reg
                    eng.reg_load(reg, p_kown[0:1, 0:1])
                return run
            P.silent(g, mkload())

        P.op("scalar", lambda s: s.activation(p_negA[:, :], p_negA[:, :], AF.Exp))
        P.wait("vector", "scalar")
        P.op("vector", lambda v: v.tensor_single_scalar(p_negA[:, :], p_negA[:, :], -1.0, ALU.mult))
        P.op("vector", lambda v: v.memset(carry[:, :], 0.0))
        P.op("vector", lambda v: v.memset(xpre[:, :], 0.0))

        def cast_load(dst_ap, src_ap, width):
            P.wait("sync", "vector")
            P.dma("sync", "wf32", w_f32[:, 0:width], src_ap)
            P.wait("vector", "wf32")
            P.op("vector", lambda v: v.tensor_copy(dst_ap, w_f32[:, 0:width]))

        for h in range(2):
            cast_load(dt_lhs[:, h * DM:(h + 1) * DM], wdtt[:, h * DM:(h + 1) * DM], DM)
        cast_load(xp_dt_l[:, 0:DM], wxp_dt[:, 0:DM], DM)
        cast_load(xp_dt_l[:, DM:2 * DM], wxp_dt[:, DM:2 * DM], DM)
        cast_load(xp_b_l[:, :], wxp_b[:, :], NDT * DS)
        cast_load(xp_c_l[:, :], wxp_c[:, :], NDT * DS)

        for i in range(NDT):
            bi = i % 2
            P.wait("sync", "vector")
            P.dma("sync", "wf32", w_f32[:, :], wx_blk[:, i * 2048:(i + 1) * 2048])
            P.wait("vector", "wf32")
            P.wait("vector", "swx%d" % bi)
            P.op("vector", lambda v, bi=bi: v.tensor_copy(w_bf[bi][:, :], w_f32[:, :]))
            P.wait("gpsimd", "vector")
            P.dma("gpsimd", "swx%d" % bi, wxbf.ap()[:, i * 2048:(i + 1) * 2048], w_bf[bi][:, :])

        def convert(src, dst, width_total):
            nch = width_total // INTER
            for i in range(nch):
                bi = i % 2
                P.wait("sync", "vector")
                P.dma("sync", "cvi%d" % bi, cv_idx[bi], src[:, i * INTER:(i + 1) * INTER])
                P.wait("vector", "cvi%d" % bi)
                P.wait("vector", "scv")
                P.op("vector", lambda v, bi=bi: v.tensor_copy(cv_ixb, cv_idx[bi]))
                P.op("vector", lambda v: v.tensor_scalar(
                    cv_acc, cv_ixb, 0.0, p_lut[:, 0:1], ALU.is_equal, ALU.mult))
                for vv in range(1, NLUT):
                    P.op("vector", lambda v, vv=vv: v.tensor_scalar(
                        cv_t, cv_ixb, float(vv), p_lut[:, vv:vv + 1],
                        ALU.is_equal, ALU.mult))
                    P.op("vector", lambda v: v.tensor_tensor(cv_acc, cv_acc, cv_t, ALU.add))
                P.wait("gpsimd", "vector")
                P.dma("gpsimd", "scv", dst.ap()[:, i * INTER:(i + 1) * INTER], cv_acc)

        convert(g_blk, wgbf, NIT * NFT * 128)
        convert(u_blk, wubf, NIT * NFT * 128)
        convert(d_blk, wdbf, NFT * NIT * 128)
        P.wait("vector", "scv")
        P.wait("sync", "swx0")
        P.wait("sync", "swx1")
        P.wait("sync", "scv")

        def ln_freedim(src, dst):
            P.op("vector", lambda v: v.reduce_sum(t_st[:, 0:1], src, axis=AX))
            P.op("scalar", lambda s: s.activation(t_lnsq[:, :], src, AF.Square))
            P.wait("vector", "scalar")
            P.op("vector", lambda v: v.reduce_sum(t_st[:, 1:2], t_lnsq[:, :], axis=AX))
            P.op("vector", lambda v: v.tensor_single_scalar(t_st[:, 2:3], t_st[:, 0:1], 1.0 / DM, ALU.mult))
            P.op("vector", lambda v: v.tensor_single_scalar(t_st[:, 3:4], t_st[:, 1:2], 1.0 / DM, ALU.mult))
            P.op("vector", lambda v: v.tensor_tensor(t_st[:, 4:5], t_st[:, 2:3], t_st[:, 2:3], ALU.mult))
            P.op("vector", lambda v: v.tensor_tensor(t_st[:, 5:6], t_st[:, 3:4], t_st[:, 4:5], ALU.subtract))
            P.op("vector", lambda v: v.tensor_single_scalar(t_st[:, 5:6], t_st[:, 5:6], EPS, ALU.add))
            P.wait("scalar", "vector")
            P.op("scalar", lambda s: s.activation(t_st[:, 6:7], t_st[:, 5:6], AF.Sqrt))
            P.wait("vector", "scalar")
            P.op("vector", lambda v: v.reciprocal(t_st[:, 7:8], t_st[:, 6:7]))
            P.op("vector", lambda v: v.tensor_tensor(t_st[:, 6:7], t_st[:, 2:3], t_st[:, 7:8], ALU.mult))
            P.op("vector", lambda v: v.tensor_single_scalar(t_st[:, 6:7], t_st[:, 6:7], -1.0, ALU.mult))
            P.op("vector", lambda v: v.tensor_scalar(
                dst, src, t_st[:, 7:8], t_st[:, 6:7], ALU.mult, ALU.add))

        for k in range(nkc):
            for tt in range(NTT):
                row0 = k * TC + tt * 128
                P.wait("sync", "vector")
                P.wait("sync", "scalar")
                P.dma("sync", "xtm", t_ln[:, :], xin[row0:row0 + 128, :])
                P.wait("vector", "xtm")
                P.wait("scalar", "xtm")
                ln_freedim(t_ln[:, :], t_lnsq[:, :])
                P.wait("vector", "sltm")
                P.op("vector", lambda v: v.tensor_copy(t_lno[:, :], t_lnsq[:, :]))
                P.wait("gpsimd", "vector")
                P.dma("gpsimd", "sltm", ln_tm.ap()[tt * 128:(tt + 1) * 128, :], t_lno[:, :])
            for ft in range(NFT):
                P.wait("sync", "tensor")
                P.wait("sync", "sltm")
                P.dma_tr("sync", "l1t", ln1t[:, ft * TC:(ft + 1) * TC],
                         ln_tm.ap()[:, ft * 128:(ft + 1) * 128])
            ln1t_ready = P._c("l1t")

            for dt in range(NDT):
                bi = dt % 2
                P.wait("sync", "tensor")
                P.dma("sync", "wbf%d" % bi, w_bf[bi][:, :],
                      wxbf.ap()[:, dt * 2048:(dt + 1) * 2048])
                P.wait("tensor", "wbf%d" % bi)
                P.wait("tensor", "l1t", ln1t_ready)
                P.wait("tensor", "vector")
                for kt in range(NFT):
                    st_, sp_ = kt == 0, kt == NFT - 1
                    if sp_:
                        P.op("tensor", (lambda t, kt=kt, bi=bi, st_=st_: mm_pair(
                            t, ps_mm, w_bf[bi][:, kt * 128:(kt + 1) * 128],
                            ln1t[:, kt * TC:(kt + 1) * TC], st_, True)), selfwait=False)
                    else:
                        P.silent("tensor", (lambda t, kt=kt, bi=bi, st_=st_: mm_pair(
                            t, ps_mm, w_bf[bi][:, kt * 128:(kt + 1) * 128],
                            ln1t[:, kt * TC:(kt + 1) * TC], st_, False)))
                P.wait("vector", "tensor")
                P.op("vector", lambda v, dt=dt: v.tensor_copy(
                    t_xr[:, 0:DC - 1], xpre[:, dt * (DC - 1):(dt + 1) * (DC - 1)]))
                P.op("vector", lambda v: v.tensor_copy(t_xr[:, DC - 1:DC - 1 + TC], ps_mm[:, :]))
                P.op("vector", lambda v, dt=dt: v.tensor_copy(
                    xpre[:, dt * (DC - 1):(dt + 1) * (DC - 1)], t_xr[:, TC:TC + DC - 1]))
                P.op("vector", lambda v, dt=dt: v.tensor_scalar(
                    t_cf[:, :], t_xr[:, 0:TC], p_convw[:, dt * DC:dt * DC + 1], None, ALU.mult))
                for j in range(1, DC):
                    P.op("vector", lambda v, j=j, dt=dt: v.scalar_tensor_tensor(
                        t_cf[:, :], t_xr[:, j:j + TC], p_convw[:, dt * DC + j:dt * DC + j + 1],
                        t_cf[:, :], ALU.mult, ALU.add))
                P.wait("scalar", "vector")
                P.wait("scalar", "sxld")
                P.wait("scalar", "tensor")
                P.op("scalar", lambda s, dt=dt: s.activation(
                    t_xld[:, :], t_cf[:, :], AF.Silu, bias=p_convb[:, dt:dt + 1]))
                P.wait("gpsimd", "scalar")
                P.dma("gpsimd", "sxld", xld_sc.ap()[dt * 128:(dt + 1) * 128, :], t_xld[:, :])
                P.wait("tensor", "scalar")
                st_, sp_ = dt == 0, dt == NDT - 1
                P.silent("tensor", (lambda t, dt=dt, st_=st_, sp_=sp_: mm_pair(
                    t, ps_d, xp_dt_l[:, dt * DTR:(dt + 1) * DTR], t_xld, st_, sp_)))
                P.silent("tensor", (lambda t, dt=dt, st_=st_, sp_=sp_: mm_pair(
                    t, ps_bc, xp_b_l[:, dt * DS:(dt + 1) * DS], t_xld, st_, sp_, pl=DS)))
                P.op("tensor", (lambda t, dt=dt, st_=st_, sp_=sp_: mm_pair(
                    t, ps_c, xp_c_l[:, dt * DS:(dt + 1) * DS], t_xld, st_, sp_,
                    pl=DS)), selfwait=False)
            P.wait("scalar", "tensor")
            P.op("scalar", lambda s: s.copy(dtlo_bf[:, :], ps_d[:, :]))
            P.op("scalar", lambda s: s.copy(bmat[:, :], ps_bc[0:DS, :]))
            P.op("scalar", lambda s: s.copy(cmat[:, :], ps_c[0:DS, :]))

            def z_builder(g, k=k):
                for dt in range(NDT):
                    bi = dt % 2
                    g.wait("sync", "vector")
                    g.dma("sync", "wf32", w_f32[:, :], wz_blk[:, dt * 2048:(dt + 1) * 2048])
                    g.wait("vector", "wf32")
                    g.wait("vector", "tensor")
                    g.op("vector", lambda v, bi=bi: v.tensor_copy(w_bf[bi][:, :], w_f32[:, :]))
                    g.wait("tensor", "vector")
                    g.wait("tensor", "scalar")
                    for kt in range(NFT):
                        if kt == NFT - 1:
                            g.op("tensor", (lambda t, kt=kt, bi=bi: mm_pair(
                                t, ps_mm, w_bf[bi][:, kt * 128:(kt + 1) * 128],
                                ln1t[:, kt * TC:(kt + 1) * TC], kt == 0, True)), selfwait=False)
                        else:
                            g.silent("tensor", (lambda t, kt=kt, bi=bi: mm_pair(
                                t, ps_mm, w_bf[bi][:, kt * 128:(kt + 1) * 128],
                                ln1t[:, kt * TC:(kt + 1) * TC], kt == 0, False)))
                    g.wait("scalar", "tensor")
                    g.wait("scalar", "ssz")
                    g.op("scalar", lambda s: s.activation(t_sz[:, :], ps_mm[:, :], AF.Silu))
                    g.wait("gpsimd", "scalar")
                    g.dma("gpsimd", "ssz", sz_sc.ap()[dt * 128:(dt + 1) * 128, :], t_sz[:, :])
            P.guarded(k, z_builder)

            P.wait("tensor", "scalar")
            for n in range(DS):
                P.op("tensor", (lambda t, n=n: mm_pair(
                    t, ps_bc, p_sel[:, n * 128:(n + 1) * 128], bmat, True, True)), selfwait=False)
                P.wait("scalar", "tensor")
                P.op("scalar", lambda s, n=n: s.copy(b_bc[:, n * TC:(n + 1) * TC], ps_bc[:, :]))
                P.wait("tensor", "scalar")
                P.op("tensor", (lambda t, n=n: mm_pair(
                    t, ps_bc, p_sel[:, n * 128:(n + 1) * 128], cmat, True, True)), selfwait=False)
                P.wait("scalar", "tensor")
                P.op("scalar", lambda s, n=n: s.copy(c_bc[:, n * TC:(n + 1) * TC], ps_bc[:, :]))
                P.wait("tensor", "scalar")

            for dt in range(NDT):
                P.wait("tensor", "scalar")
                P.wait("tensor", "vector")
                P.op("tensor", (lambda t, dt=dt: mm_pair(
                    t, ps_d, dt_lhs[:, dt * 128:(dt + 1) * 128], dtlo_bf, True, True)), selfwait=False)
                P.wait("scalar", "tensor")
                P.wait("scalar", "vector")
                P.op("scalar", lambda s, dt=dt: s.activation(
                    t_sp[:, :], ps_d[:, :], AF.Abs, bias=p_dtb[:, dt:dt + 1]))
                P.op("scalar", lambda s: s.activation(t_sp[:, :], t_sp[:, :], AF.Exp, scale=-1.0))
                P.op("scalar", lambda s: s.activation(t_sp[:, :], t_sp[:, :], AF.Ln, bias=1.0))
                P.wait("vector", "scalar")
                P.op("vector", lambda v, dt=dt: v.tensor_scalar(
                    t_cf[:, :], ps_d[:, :], p_dtb[:, dt:dt + 1], 0.0, ALU.add, ALU.max))
                P.op("vector", lambda v: v.tensor_tensor(t_dl[:, :], t_cf[:, :], t_sp[:, :], ALU.add))
                P.wait("sync", "vector")
                P.wait("sync", "tensor")
                P.wait("sync", "sxld")
                P.dma("sync", "lxld", t_xld[:, :], xld_sc.ap()[dt * 128:(dt + 1) * 128, :])
                P.wait("vector", "lxld")
                P.op("vector", lambda v: v.tensor_tensor(t_u[:, :], t_dl[:, :], t_xld[:, :], ALU.mult))
                for n in range(DS):
                    col = dt * DS + n
                    P.wait("scalar", "vector")
                    P.op("scalar", lambda s, col=col: s.activation(
                        t_a[:, :], t_dl[:, :], AF.Exp, scale=p_negA[:, col:col + 1]))
                    P.wait("vector", "scalar")
                    P.op("vector", lambda v, n=n: v.tensor_tensor(
                        t_b[:, :], t_u[:, :], b_bc[:, n * TC:(n + 1) * TC], ALU.mult))
                    P.op("vector", lambda v, col=col: v.tensor_tensor_scan(
                        t_h[:, :], t_a[:, :], t_b[:, :], carry[:, col:col + 1],
                        ALU.mult, ALU.add))
                    P.op("vector", lambda v, col=col: v.tensor_copy(
                        carry[:, col:col + 1], t_h[:, TC - 1:TC]))

                    def own_scan(g, n=n):
                        g.op("vector", lambda v, n=n: v.tensor_tensor(
                            t_ch[:, :], t_h[:, :], c_bc[:, n * TC:(n + 1) * TC], ALU.mult))
                        if n == 0:
                            g.op("vector", lambda v: v.tensor_copy(t_y[:, :], t_ch[:, :]))
                        else:
                            g.op("vector", lambda v: v.tensor_tensor(
                                t_y[:, :], t_y[:, :], t_ch[:, :], ALU.add))
                    P.guarded(k, own_scan)

                def own_gate(g, dt=dt):
                    g.wait("gpsimd", "vector")
                    g.wait("gpsimd", "ssz")
                    g.dma("gpsimd", "lsz", t_sz[:, :], sz_sc.ap()[dt * 128:(dt + 1) * 128, :])
                    g.op("vector", lambda v, dt=dt: v.scalar_tensor_tensor(
                        t_y[:, :], t_xld[:, :], p_dvec[:, dt:dt + 1], t_y[:, :],
                        ALU.mult, ALU.add))
                    g.wait("vector", "lsz")
                    g.wait("vector", "syg")
                    g.op("vector", lambda v: v.tensor_tensor(t_yg[:, :], t_y[:, :], t_sz[:, :], ALU.mult))
                    g.wait("gpsimd", "vector")
                    g.dma("gpsimd", "syg", ygate_sc.ap()[dt * 128:(dt + 1) * 128, :], t_yg[:, :])
                P.guarded(k, own_gate)

        syg_total = P._c("syg")
        ssz_total = P._c("ssz")

        for mtp in range(NFT // 2):
            for half, psum in ((0, ps_mm), (1, ps_d)):
                mt = mtp * 2 + half
                P.wait("tensor", "scalar")
                for dt in range(NDT):
                    bi = dt % 2
                    P.wait("sync", "vector")
                    P.dma("sync", "wf32", w_f32[:, 0:128],
                          wot_blk[:, dt * 2048 + mt * 128:dt * 2048 + (mt + 1) * 128])
                    P.wait("sync", "tensor")
                    P.wait("sync", "syg", syg_total)
                    P.dma("sync", "yt%d" % bi, t_yt[bi][:, :],
                          ygate_sc.ap()[dt * 128:(dt + 1) * 128, :])
                    P.wait("vector", "wf32")
                    P.op("vector", lambda v, bi=bi: v.tensor_copy(w_bf[bi][:, 0:128], w_f32[:, 0:128]))
                    P.wait("tensor", "vector")
                    P.wait("tensor", "yt%d" % bi)
                    P.op("tensor", (lambda t, bi=bi, psum=psum, dt=dt: mm_pair(
                        t, psum, w_bf[bi][:, 0:128], t_yt[bi], dt == 0, dt == NDT - 1)),
                        selfwait=False)
                P.wait("scalar", "tensor")
                P.wait("scalar", "sfm")
                P.op("scalar", lambda s, psum=psum: s.copy(t_ev[:, :], psum[:, :]))
                P.wait("gpsimd", "scalar")
                P.dma("gpsimd", "sfm", fm_sc.ap()[mt * 128:(mt + 1) * 128, :], t_ev[:, :])
        P.wait("sync", "ssz", ssz_total)

        for tt in range(NTT):
            P.wait("sync", "sfm")
            P.wait("sync", "scalar")
            P.wait("sync", "vector")
            P.wait("sync", "sltm")
            P.wait("sync", "sx2")
            P.dma_tr("sync", "tlo", t_lno[:, :], fm_sc.ap()[:, tt * 128:(tt + 1) * 128])
            P.dma("sync", "xtm", t_ln[:, :], xmy[tt * 128:(tt + 1) * 128, :])
            P.wait("vector", "tlo")
            P.wait("vector", "xtm")
            P.wait("scalar", "tlo")
            ln_freedim(t_lno[:, :], t_lnsq[:, :])
            P.wait("vector", "sx2")
            P.op("vector", lambda v: v.tensor_tensor(t_ln[:, :], t_ln[:, :], t_lnsq[:, :], ALU.add))
            P.wait("gpsimd", "vector")
            P.dma("gpsimd", "sx2", x2_sc.ap()[tt * 128:(tt + 1) * 128, :], t_ln[:, :])
            P.wait("scalar", "vector")
            ln_freedim(t_ln[:, :], t_lnsq[:, :])
            P.wait("vector", "sltm")
            P.op("vector", lambda v: v.tensor_copy(t_lno[:, :], t_lnsq[:, :]))
            P.wait("gpsimd", "vector")
            P.dma("gpsimd", "sltm", ln_tm.ap()[tt * 128:(tt + 1) * 128, :], t_lno[:, :])
        for ft in range(NFT):
            P.wait("sync", "sltm")
            P.wait("sync", "tensor")
            P.dma_tr("sync", "l1t", ln2t[:, ft * TC:(ft + 1) * TC],
                     ln_tm.ap()[:, ft * 128:(ft + 1) * 128])
        ln2t_ready = P._c("l1t")

        for mt in range(NIT):
            bi = mt % 2
            P.wait("sync", "tensor")
            P.dma("sync", "wbf%d" % bi, w_bf[bi][:, :], wgbf.ap()[:, mt * 2048:(mt + 1) * 2048])
            P.wait("sync", "scalar")
            P.wait("sync", "vector")
            P.dma("sync", "tlo", t_lno[:, :], wubf.ap()[:, mt * 2048:(mt + 1) * 2048])
            P.wait("tensor", "wbf%d" % bi)
            P.wait("tensor", "tlo")
            P.wait("tensor", "l1t", ln2t_ready)
            P.wait("tensor", "vector")
            P.wait("tensor", "scalar")
            for kt in range(NFT):
                if kt == NFT - 1:
                    P.op("tensor", (lambda t, kt=kt, bi=bi: mm_pair(
                        t, ps_mm, w_bf[bi][:, kt * 128:(kt + 1) * 128],
                        ln2t[:, kt * TC:(kt + 1) * TC], kt == 0, True)), selfwait=False)
                    P.op("tensor", (lambda t, kt=kt: mm_pair(
                        t, ps_d, t_lno[:, kt * 128:(kt + 1) * 128],
                        ln2t[:, kt * TC:(kt + 1) * TC], kt == 0, True)), selfwait=False)
                else:
                    P.silent("tensor", (lambda t, kt=kt, bi=bi: mm_pair(
                        t, ps_mm, w_bf[bi][:, kt * 128:(kt + 1) * 128],
                        ln2t[:, kt * TC:(kt + 1) * TC], kt == 0, False)))
                    P.silent("tensor", (lambda t, kt=kt: mm_pair(
                        t, ps_d, t_lno[:, kt * 128:(kt + 1) * 128],
                        ln2t[:, kt * TC:(kt + 1) * TC], kt == 0, False)))
            P.wait("scalar", "tensor")
            P.wait("scalar", "vector")
            P.op("scalar", lambda s: s.activation(t_ev[:, :], ps_mm[:, :], AF.Silu))
            P.wait("vector", "scalar")
            P.wait("vector", "sh")
            P.op("vector", lambda v: v.tensor_tensor(t_yg[:, :], t_ev[:, :], ps_d[:, :], ALU.mult))
            P.wait("gpsimd", "vector")
            P.dma("gpsimd", "sh", h_sc.ap()[mt * 128:(mt + 1) * 128, :], t_yg[:, :])
        sh_total = P._c("sh")

        for m2p in range(NFT // 2):
            for half, psum in ((0, ps_mm), (1, ps_d)):
                mt2 = m2p * 2 + half
                P.wait("tensor", "scalar")
                P.wait("sync", "tensor")
                P.dma("sync", "dnl", dn_l, wdbf.ap()[:, mt2 * (NIT * 128):(mt2 + 1) * (NIT * 128)])
                for it in range(NIT):
                    bi = it % 2
                    P.wait("sync", "tensor")
                    P.wait("sync", "sh", sh_total)
                    P.dma("sync", "yt%d" % bi, t_yt[bi][:, :], h_sc.ap()[it * 128:(it + 1) * 128, :])
                    P.wait("tensor", "yt%d" % bi)
                    P.wait("tensor", "dnl")
                    P.op("tensor", (lambda t, it=it, bi=bi, psum=psum: mm_pair(
                        t, psum, dn_l[:, it * 128:(it + 1) * 128], t_yt[bi], it == 0, it == NIT - 1)),
                        selfwait=False)
                P.wait("scalar", "tensor")
                P.wait("scalar", "sfm")
                P.op("scalar", lambda s, psum=psum: s.copy(t_ev[:, :], psum[:, :]))
                P.wait("gpsimd", "scalar")
                P.dma("gpsimd", "sfm", fm_sc.ap()[mt2 * 128:(mt2 + 1) * 128, :], t_ev[:, :])
        sfm_total = P._c("sfm")

        for tt in range(NTT):
            P.wait("sync", "sfm", sfm_total)
            P.wait("sync", "scalar")
            P.wait("sync", "vector")
            P.wait("sync", "sout")
            P.wait("sync", "sltm")
            P.wait("sync", "sx2")
            P.dma_tr("sync", "tlo", t_lno[:, :], fm_sc.ap()[:, tt * 128:(tt + 1) * 128])
            P.dma("sync", "xtm", t_ln[:, :], x2_sc.ap()[tt * 128:(tt + 1) * 128, :])
            P.wait("vector", "tlo")
            P.wait("vector", "xtm")
            P.wait("scalar", "tlo")
            ln_freedim(t_lno[:, :], t_lnsq[:, :])
            P.op("vector", lambda v: v.tensor_tensor(t_ln[:, :], t_ln[:, :], t_lnsq[:, :], ALU.add))
            P.wait("sync", "vector")
            P.dma("sync", "sout", yout[tt * 128:(tt + 1) * 128, :], t_ln[:, :])
        P.wait("sync", "sout")

        for name in list(P.cnt):
            P.sems[name] = e(nc.semaphore("S_" + name))

        block = e(nc.Block())

        @block.sync
        def _(sync):
            for fn in P.q["sync"]:
                fn(sync)

        @block.tensor
        def _(tensor):
            for fn in P.q["tensor"]:
                fn(tensor)

        @block.vector
        def _(vector):
            for fn in P.q["vector"]:
                fn(vector)

        @block.scalar
        def _(scalar):
            for fn in P.q["scalar"]:
                fn(scalar)

        @block.gpsimd
        def _(gpsimd):
            for fn in P.q["gpsimd"]:
                fn(gpsimd)

    return nc


_NC_CACHE = {}


def _host_layout(inputs):
    import ml_dtypes
    x = np.ascontiguousarray(np.asarray(inputs["L0_attn_hidden_states"], np.float32).reshape(L, DM))
    wip = np.asarray(inputs["L0_attn_in_proj_w"], np.float32)
    xp = np.asarray(inputs["L0_attn_x_proj_w"], np.float32)

    def blk_w(wT):
        return np.ascontiguousarray(
            wT.reshape(NFT, 128, NDT, 128).transpose(1, 2, 0, 3).reshape(128, NDT * 2048))

    def blk_di(w):
        wd = w.shape[1]
        return np.ascontiguousarray(
            w.reshape(NDT, 128, wd).transpose(1, 0, 2).reshape(128, NDT * wd))

    wxT = wip[0:DI].T
    wzT = wip[DI:2 * DI].T
    woT = np.asarray(inputs["L0_attn_out_proj_w"], np.float32).T
    wot_blk = np.ascontiguousarray(
        woT.reshape(NDT, 128, NFT, 128).transpose(1, 0, 2, 3).reshape(128, NDT * 2048))

    gi = np.asarray(inputs["L0_ffn_gate_idx"], np.int32)
    ui = np.asarray(inputs["L0_ffn_up_idx"], np.int32)
    di = np.asarray(inputs["L0_ffn_down_idx"], np.int32)
    g_blk = np.ascontiguousarray(
        gi.reshape(NIT, 128, NFT, 128).transpose(3, 0, 2, 1).reshape(128, NIT * NFT * 128))
    u_blk = np.ascontiguousarray(
        ui.reshape(NIT, 128, NFT, 128).transpose(3, 0, 2, 1).reshape(128, NIT * NFT * 128))
    d_blk = np.ascontiguousarray(
        di.reshape(NFT, 128, NIT, 128).transpose(3, 0, 2, 1).reshape(128, NFT * NIT * 128))

    sel = np.zeros((DS, DS * 128), ml_dtypes.bfloat16)
    for n in range(DS):
        sel[n, n * 128:(n + 1) * 128] = 1.0

    base = {
        "xin": x,
        "wx_blk": blk_w(wxT),
        "wz_blk": blk_w(wzT),
        "wot_blk": wot_blk,
        "wxp_dt": blk_di(xp[0:DTR].T.reshape(DI, DTR)),
        "wxp_b": blk_di(xp[DTR:DTR + DS].T.reshape(DI, DS)),
        "wxp_c": blk_di(xp[DTR + DS:DTR + 2 * DS].T.reshape(DI, DS)),
        "wdtt": np.ascontiguousarray(np.asarray(inputs["L0_attn_dt_proj_w"], np.float32).T),
        "p_convw_in": blk_di(np.asarray(inputs["L0_attn_conv_w"], np.float32)),
        "p_convb_in": blk_di(np.asarray(inputs["L0_attn_conv_b"], np.float32).reshape(DI, 1)),
        "p_dtb_in": blk_di(np.asarray(inputs["L0_attn_dt_proj_b"], np.float32).reshape(DI, 1)),
        "p_alog_in": blk_di(np.asarray(inputs["L0_attn_A_log"], np.float32)),
        "p_dvec_in": blk_di(np.asarray(inputs["L0_attn_D"], np.float32).reshape(DI, 1)),
        "lutv": np.ascontiguousarray(np.broadcast_to(np.asarray(inputs["L0_ffn_lut"], np.float32).reshape(1, NLUT), (128, NLUT))),
        "g_blk": g_blk,
        "u_blk": u_blk,
        "d_blk": d_blk,
        "selmat": sel,
    }
    return base, x


def kernel(**inputs):
    base, x = _host_layout(inputs)
    if "nc" not in _NC_CACHE:
        _NC_CACHE["nc"] = build_program()
    nc = _NC_CACHE["nc"]
    in_maps = []
    for c in range(8):
        m = dict(base)
        m["xmy"] = np.ascontiguousarray(x[c * TC:(c + 1) * TC])
        m["kown_in"] = np.array([[c]], np.int32)
        in_maps.append(m)
    res = run_bass_kernel_spmd(nc, in_maps, core_ids=list(range(8)))
    out = np.concatenate([np.asarray(res.results[c]["yout"], np.float32) for c in range(8)], axis=0)
    return out.reshape(1, L, DM)


if __name__ == "__main__":
    import reference as _r
    ins = _r.setup_inputs(0)
    out = kernel(**{k: np.asarray(v) for k, v in ins.items()})
    ref = np.asarray(_r.reference(**ins))
    err = np.linalg.norm(out - ref) / np.linalg.norm(ref)
    print("rel err:", err)
